# version 17
# baseline (speedup 1.0000x reference)
import numpy as np
import contextlib
import concourse.bass as bass
import concourse.mybir as mybir
from concourse.bass_utils import run_bass_kernel_spmd

F32 = mybir.dt.float32
BF16 = mybir.dt.bfloat16
U8 = mybir.dt.uint8
Alu = mybir.AluOpType
Act = mybir.ActivationFunctionType

P = 128
D = 2048
KC = 16
T = 1024
TH = 512
NCORES = 8
NMLP = 16384
NGRP = 16
EPS = 1e-5

SEM_NAMES = ["sx", "swa", "spk", "sdq", "slp", "slpr", "sexg", "scc", "sbd",
             "sout", "v", "se", "mm", "gp"]
ENGS = ["sync", "vector", "scalar", "tensor", "gpsimd"]


class _RecInstr:
    __slots__ = ("incs",)

    def __init__(self):
        self.incs = []

    def then_inc(self, sem_name, k):
        self.incs.append((sem_name, k))
        return self


class _EngProxy:
    def __init__(self, B, eng):
        self.B = B
        self.eng = eng

    def wait(self, sem, key):
        val = key if isinstance(key, int) else self.B.rec_get(key)
        if self.B.emitting and val > 0:
            self.B.stream[self.eng].append(("w", sem, val))

    def op(self, name, *args, **kwargs):
        ri = _RecInstr()
        if self.B.emitting:
            self.B.stream[self.eng].append(("o", name, args, kwargs, ri))
        return _Chain(self.B, ri)

    def __getattr__(self, name):
        return lambda *a, **k: self.op(name, *a, **k)


class _Chain:
    __slots__ = ("B", "ri")

    def __init__(self, B, ri):
        self.B = B
        self.ri = ri

    def inc(self, sem, k=1):
        self.ri.then_inc(sem, k)
        self.B.cnt[sem] += k
        return self.B.cnt[sem]

    def dma(self, sem):
        return self.inc(sem, 16)


class _Builder:
    def __init__(self, emitting, rec):
        self.emitting = emitting
        self.rec = rec
        self.cnt = {s: 0 for s in SEM_NAMES}
        self.stream = {e: [] for e in ENGS}
        self.proxies = {e: _EngProxy(self, e) for e in ENGS}

    def __getattr__(self, name):
        if name in ENGS:
            return self.proxies[name]
        raise AttributeError(name)

    def note(self, key, sem):
        if not self.emitting:
            self.rec[key] = self.cnt[sem]
        else:
            assert self.rec.get(key) == self.cnt[sem], \
                f"pass mismatch at {key}: {self.rec.get(key)} != {self.cnt[sem]}"

    def rec_get(self, key):
        if not self.emitting:
            return 0
        assert key in self.rec, f"missing note {key}"
        return self.rec[key]

    def n(self, sem):
        return self.cnt[sem]


def build_program():
    nc = bass.Bass("TRN2", target_bir_lowering=False, debug=False,
                   detect_race_conditions=False)

    xin = nc.dram_tensor("xin", [D, T], F32, kind="ExternalInput")
    wti = nc.dram_tensor("wti", [KC, P, KC, P], F32, kind="ExternalInput")
    wtd = nc.dram_tensor("wtd", [KC, P, KC, P], F32, kind="ExternalInput")
    wtg = nc.dram_tensor("wtg", [KC, P, KC, P], F32, kind="ExternalInput")
    wto = nc.dram_tensor("wto", [KC, P, KC, P], F32, kind="ExternalInput")
    bin_d = nc.dram_tensor("bin_d", [P, KC], F32, kind="ExternalInput")
    bg_d = nc.dram_tensor("bg_d", [P, KC], F32, kind="ExternalInput")
    bd_d = nc.dram_tensor("bd_d", [P, KC], F32, kind="ExternalInput")
    db_d = nc.dram_tensor("db_d", [P, KC], F32, kind="ExternalInput")
    bo_d = nc.dram_tensor("bo_d", [P, KC], F32, kind="ExternalInput")
    wgs = nc.dram_tensor("wgs", [P, NMLP], U8, kind="ExternalInput")
    wus = nc.dram_tensor("wus", [P, NMLP], U8, kind="ExternalInput")
    wds = nc.dram_tensor("wds", [NGRP, 64, D], U8, kind="ExternalInput")
    gs_d = nc.dram_tensor("gs_d", [8, NMLP], F32, kind="ExternalInput")
    us_d = nc.dram_tensor("us_d", [8, NMLP], F32, kind="ExternalInput")
    ds_d = nc.dram_tensor("ds_d", [NGRP, 4, D], F32, kind="ExternalInput")
    msk_d = nc.dram_tensor("msk_d", [P, 8], F32, kind="ExternalInput")
    out_d = nc.dram_tensor("out_d", [D, T], F32, kind="ExternalOutput")

    lp_dram = nc.dram_tensor("lp_dram", [2, KC, P, T], BF16)
    deq_in = [nc.dram_tensor(f"deq_in{g}", [6, P, 1024], BF16)
              for g in range(NGRP)]
    ag_out = [nc.dram_tensor(f"ag_out{g}", [NCORES, 6, P, 1024], BF16,
                             addr_space="Shared") for g in range(NGRP)]
    ex_in = nc.dram_tensor("ex_in", [P, 32], F32)
    ex_out = nc.dram_tensor("ex_out", [NCORES, P, 32], F32, addr_space="Shared")

    def dpan(g, p):
        return deq_in[g][p]

    def agpan(g, rank, p):
        return ag_out[g][rank, p]

    est = contextlib.ExitStack()
    with est:
        sbt = lambda name, shape, dt: est.enter_context(nc.sbuf_tensor(name, shape, dt))
        x_sb = sbt("x_sb", [P, KC * T], F32)
        xn_sb = sbt("xn_sb", [P, KC * T], BF16)
        bin_sb = sbt("bin_sb", [P, KC], F32)
        bg_sb = sbt("bg_sb", [P, KC], F32)
        bd_sb = sbt("bd_sb", [P, KC], F32)
        db_sb = sbt("db_sb", [P, KC], F32)
        bo_sb = sbt("bo_sb", [P, KC], F32)
        bds_sb = sbt("bds_sb", [P, KC], F32)
        msk_sb = sbt("msk_sb", [P, 8], F32)
        ones_k = sbt("ones_k", [P, 1], BF16)
        ones_1 = sbt("ones_1", [1, P], BF16)
        fence_sb = sbt("fence_sb", [P, 4], F32)
        dstage = [sbt(f"dstage{i}", [P, 1024], U8) for i in range(2)]
        dbc = [sbt(f"dbc{i}", [P, TH], F32) for i in range(2)]
        dklo = sbt("dklo", [P, 256], U8)
        dslo = sbt("dslo", [P, 256], U8)
        dt1 = sbt("dt1", [P, 256], BF16)
        du = sbt("du", [P, 256], BF16)
        dw_ = sbt("dw_", [P, 256], BF16)
        dq = sbt("dq", [P, 256], BF16)
        dwout = [sbt(f"dwout{i}", [P, TH], BF16) for i in range(2)]
        attn_sc = contextlib.ExitStack()
        asbt = lambda name, shape, dt: attn_sc.enter_context(nc.sbuf_tensor(name, shape, dt))
        y_sb = asbt("y_sb", [P, KC * T], BF16)
        em_t = asbt("em_t", [P, T], F32)
        zeros_t = asbt("zeros_t", [P, T], BF16)
        ex_sb = asbt("ex_sb", [P, 32], F32)
        exa_sb = asbt("exa_sb", [P, 256], F32)
        S_sb = asbt("S_sb", [P, 16], F32)
        Tt_sb = asbt("Tt_sb", [P, 16], F32)
        Dt_sb = asbt("Dt_sb", [P, 16], F32)
        wst = [asbt(f"wst{i}", [P, KC * P], F32) for i in range(2)]
        wbf = [asbt(f"wbf{i}", [P, KC * P], BF16) for i in range(4)]
        inp_t = [asbt(f"inp_t{i}", [P, T], F32) for i in range(2)]
        dec_t = [asbt(f"dec_t{i}", [P, T], F32) for i in range(2)]
        Lt = [asbt(f"Lt{i}", [P, T], BF16) for i in range(2)]
        Pt = [asbt(f"Pt{i}", [P, T], BF16) for i in range(2)]
        gtmp = [asbt(f"gtmp{i}", [P, T], BF16) for i in range(2)]
        xbf = [y_sb[:, 0:T]]
        xsq = [y_sb[:, T:2 * T]]
        rstd_b = y_sb[:, 2 * T:3 * T]
        mrr_b = y_sb[:, 3 * T:4 * T]
        rstd_bf = y_sb[0:1, 4 * T:5 * T]
        mr_bf = y_sb[0:1, 5 * T:6 * T]
        ta_s = em_t[0:1, :]
        attn_sc.close()
        NWR = 6
        wrng = [sbt(f"wrng{i}", [P, KC * 256], BF16) for i in range(NWR)]
        Wdb = sbt("Wdb", [P, 8 * D], BF16)
        inter = sbt("inter", [P, 8 * T], BF16)
        gel_t = [sbt(f"gel_t{i}", [P, TH], BF16) for i in range(2)]

        psA = est.enter_context(nc.psum_tensor("psA", [P, TH], F32))
        psB = est.enter_context(nc.psum_tensor("psB", [P, TH], F32))
        psC = est.enter_context(nc.psum_tensor("psC", [P, TH], F32))
        psD2 = est.enter_context(nc.psum_tensor("psD2", [P, TH], F32))
        psE = est.enter_context(nc.psum_tensor("psE", [P, TH], F32))
        psF = est.enter_context(nc.psum_tensor("psF", [P, TH], F32))
        psGb = est.enter_context(nc.psum_tensor("psGb", [P, TH], F32))
        psHb = est.enter_context(nc.psum_tensor("psHb", [P, TH], F32))
        psI4 = [[psA, psB], [psGb, psHb]]
        psDc4 = [[psC, psD2], [psE, psF]]
        psO4 = [[psE, psF], [psC, psD2]]
        psG = [psA, psB]
        psU = [psC, psD2]
        psDn4 = [[psE, psF], [psGb, psHb]]
        psS4 = [psGb, psHb, psE, psF]
        psBC = [psA, psB]

        sems = {name: est.enter_context(nc.semaphore(name)) for name in SEM_NAMES}
        block = est.enter_context(nc.Block())

        xch = lambda kc: x_sb[:, kc * T:(kc + 1) * T]
        xnch = lambda kc: xn_sb[:, kc * T:(kc + 1) * T]
        ych = lambda kc: y_sb[:, kc * T:(kc + 1) * T]

        deq_slices = []
        sid = 0
        for g in range(NGRP):
            for w, (wsrc, pbase) in enumerate((("wg", 0), ("wu", 2))):
                for s in range(2):
                    deq_slices.append(dict(kind="gu", w=wsrc, g=g, s=s, pbase=pbase,
                                           stage=sid, fresh=True))
                    sid += 1
            for s in range(2):
                deq_slices.append(dict(kind="wd", w="wd", g=g, s=s, pbase=4,
                                       stage=sid if s == 0 else sid - 1, fresh=(s == 0)))
                if s == 0:
                    sid += 1
        NSL = len(deq_slices)

        wsteps = []
        for ec in range(KC):
            wsteps.append(("wti", ec))
            wsteps.append(("wtd", ec))
        for ec in range(KC):
            wsteps.append(("wtg", ec))
        for dc in range(KC):
            wsteps.append(("wto", dc))
        wt_map = {"wti": wti, "wtd": wtd, "wtg": wtg, "wto": wto}
        NWS = len(wsteps)

        def emit(B):
            sync, vector, scalar, tensor, gpsimd = (B.sync, B.vector, B.scalar,
                                                    B.tensor, B.gpsimd)
            vs_row = inp_t[0][0:1, :]
            sq_r = [psGb[0:1, :], psHb[0:1, :]]
            var_r = [psC[0:1, :], psD2[0:1, :]]

            for kc in range(KC):
                sync.dma_start(out=x_sb[:, kc * T:(kc + 1) * T],
                               in_=xin[kc * P:(kc + 1) * P, :]).dma("sx")
                B.note(f"sx{kc}", "sx")
            for dst, src in ((bin_sb, bin_d), (bg_sb, bg_d), (bd_sb, bd_d),
                             (db_sb, db_d), (bo_sb, bo_d), (msk_sb, msk_d)):
                sync.dma_start(out=dst[:, :], in_=src[:, :]).dma("sx")
            B.note("sx_all", "sx")

            def wstep_load(i):
                wname, ec = wsteps[i]
                if i >= 2:
                    sync.wait("se", f"wconv{i-2}")
                sync.dma_start(out=wst[i % 2][:, :], in_=wt_map[wname][ec]).dma("swa")
                B.note(f"wload{i}", "swa")

            def lp_reload(r):
                sync.wait("slp", "lspill_all")
                if r >= 2:
                    sync.wait("v", f"ecy{r-2}")
                sync.dma_start(out=Pt[r % 2][:, :], in_=lp_dram[0, r]).dma("slpr")
                sync.dma_start(out=Lt[r % 2][:, :], in_=lp_dram[1, r]).dma("slpr")
                B.note(f"lpre{r}", "slpr")

            for i in range(2 * KC):
                wstep_load(i)
            for i in range(2 * KC, 3 * KC):
                wstep_load(i)
                if i - 34 >= 0:
                    lp_reload(i - 34)
            lp_reload(KC - 2)
            lp_reload(KC - 1)
            for i in range(3 * KC, NWS):
                wstep_load(i)
            sync.dma_start(out=fence_sb[:, :], in_=bin_d[:, 0:4]).dma("swa")
            B.note("wl_tail", "swa")

            sync.wait("v", "v_attn_done")
            for g in range(NGRP):
                sync.wait("scc", f"cc{g}")
                pan = ag_out[g].ap().rearrange("r s q t -> r (s q) t")
                for nc4 in range(4):
                    for mat, pb in ((0, 0), (1, 2)):
                        jg = g * 8 + 2 * nc4 + mat
                        if jg >= NWR:
                            ju = jg - NWR
                            g2, nc4p = ju // 8, (ju % 8) // 2
                            sync.wait("mm", f"guU{g2*16 + 4*nc4p + 3}")
                        slot = jg % NWR
                        dst3 = wrng[slot].ap().rearrange("p (c n) -> p c n", c=8)
                        for e in range(2):
                            sync.dma_start(
                                out=dst3[:, :, e * 256:(e + 1) * 256],
                                in_=pan[:, (pb + e) * P:(pb + e + 1) * P,
                                        nc4 * 256:(nc4 + 1) * 256]
                                    .transpose([1, 0, 2])).dma("sbd")
                        B.note(f"fgl{jg}", "sbd")
                sync.dma_start(out=fence_sb[:, :], in_=bin_d[:, 0:4]).dma("sbd")
                B.note(f"fguf{g}", "sbd")

            def emit_group_cc(g):
                gpsimd.wait("sdq", f"dgrp{g}")
                gpsimd.collective_compute(
                    "AllGather", Alu.bypass, replica_groups=[list(range(NCORES))],
                    ins=[deq_in[g].ap().opt()], outs=[ag_out[g].ap().opt()]).inc("scc")
                B.note(f"cc{g}", "scc")

            def emit_exchange():
                gpsimd.wait("v", "scans_done")
                gpsimd.dma_start(out=ex_in[:, :], in_=ex_sb[:, :]).dma("sexg")
                B.note("exin", "sexg")
                gpsimd.wait("sexg", "exin")
                gpsimd.collective_compute(
                    "AllGather", Alu.bypass, replica_groups=[list(range(NCORES))],
                    ins=[ex_in.ap().opt()], outs=[ex_out.ap().opt()]).inc("scc")
                B.note("ccex", "scc")
                gpsimd.wait("scc", "ccex")
                gpsimd.dma_start(
                    out=exa_sb.ap().rearrange("p (r k) -> p r k", r=NCORES),
                    in_=ex_out.ap().transpose([1, 0, 2])).dma("sexg")
                gpsimd.dma_start(out=fence_sb[:, :], in_=gs_d[0:8, 0:4]
                                 [:, None, :].broadcast_to([8, 16, 4])).dma("sexg")
                B.note("exall", "sexg")

            for i, sl in enumerate(deq_slices):
                if sl["fresh"]:
                    slot = sl["stage"] % 2
                    prev = sl["stage"] - 2
                    if prev >= 0:
                        gpsimd.wait("v", f"dv_stage{prev}")
                    if sl["kind"] == "gu":
                        g, s = sl["g"], sl["s"]
                        o = g * 1024 + s * TH
                        wsrc = wgs if sl["w"] == "wg" else wus
                        gpsimd.dma_start(out=dstage[slot][:, 0:TH],
                                         in_=wsrc[:, o:o + TH]).dma("spk")
                    else:
                        g = sl["g"]
                        gpsimd.dma_start(
                            out=dstage[slot][:, :],
                            in_=wds[g].rearrange("r (h dl) -> r h dl", h=2)
                                .transpose([1, 0, 2])).dma("spk")
                if i >= 2:
                    gpsimd.wait("v", f"dvex{i-2}")
                if sl["kind"] == "gu":
                    g, s = sl["g"], sl["s"]
                    o = g * 1024 + s * TH
                    ssrc = gs_d if sl["w"] == "wg" else us_d
                    gpsimd.dma_start(
                        out=dbc[i % 2][:, :],
                        in_=ssrc[:, o:o + TH][:, None, :].broadcast_to([8, 16, TH])
                    ).dma("spk")
                else:
                    g, s = sl["g"], sl["s"]
                    for h in range(2):
                        gpsimd.dma_start(
                            out=dbc[i % 2][h * 64:(h + 1) * 64, :],
                            in_=ds_d[g][:, h * 1024 + s * TH: h * 1024 + (s + 1) * TH]
                                [:, None, :].broadcast_to([4, 16, TH])
                        ).dma("spk")
                B.note(f"dload{i}", "spk")
                if i == NSL - 1:
                    gpsimd.dma_start(out=fence_sb[:, :], in_=gs_d[0:8, 0:4]
                                     [:, None, :].broadcast_to([8, 16, 4])).dma("spk")
                    B.note("dload_tail", "spk")
                if i >= 1:
                    gpsimd.wait("v", f"dvex{i-1}")
                    slp_ = deq_slices[i - 1]
                    for half in range(2):
                        gpsimd.dma_start(
                            out=dpan(slp_["g"], slp_["pbase"] + half)
                                [:, slp_["s"] * TH:(slp_["s"] + 1) * TH],
                            in_=dwout[half][:, :]).dma("sdq")
                    B.note(f"dwr{i-1}", "sdq")
                    if (i - 1) % 6 == 5:
                        gdone = (i - 1) // 6
                        B.note(f"dgrp{gdone}", "sdq")
                        emit_group_cc(gdone)
                    if i - 1 == 9:
                        emit_exchange()

            gpsimd.wait("v", f"dvex{NSL-1}")
            slp_ = deq_slices[NSL - 1]
            for half in range(2):
                gpsimd.dma_start(
                    out=dpan(slp_["g"], slp_["pbase"] + half)
                        [:, slp_["s"] * TH:(slp_["s"] + 1) * TH],
                    in_=dwout[half][:, :]).dma("sdq")
            B.note(f"dwr{NSL-1}", "sdq")
            B.note(f"dgrp{(NSL-1) // 6}", "sdq")
            emit_group_cc((NSL - 1) // 6)
            gpsimd.wait("v", "ffn_done")
            gpsimd.dma_start(
                out=out_d.ap().rearrange("(k p) t -> k p t", p=P).transpose([1, 0, 2]),
                in_=x_sb.ap().rearrange("p (k t) -> p k t", k=KC)).dma("sout")
            gpsimd.wait("sout", 16)
            gpsimd.wait("slp", "lspill_all")

            dq_i = [0]
            dq_seg = [0]

            def deq_seg(nseg):
                for _ in range(nseg):
                    i = dq_i[0]
                    if i >= NSL:
                        return
                    sl = deq_slices[i]
                    slot = sl["stage"] % 2
                    base = 0 if sl["kind"] == "gu" else sl["s"] * TH
                    s4 = dq_seg[0]
                    h, q = s4 // 2, s4 % 2
                    qs = slice(q * 256, (q + 1) * 256)
                    dsrc = dstage[slot][:, base + q * 256: base + (q + 1) * 256]
                    if s4 == 0:
                        vector.wait("spk", f"dload{i+1}" if i + 1 < NSL else "dload_tail")
                        if i >= 1:
                            vector.wait("sdq", f"dwr{i-1}")
                    kk = dklo[:, 0:256]
                    ss2 = dslo[:, 0:256]
                    if h == 0:
                        vector.tensor_scalar(kk, dsrc, 7, None, Alu.bitwise_and)
                        vector.tensor_scalar(ss2, dsrc, 8, None, Alu.bitwise_and)
                    else:
                        vector.tensor_scalar(kk, dsrc, 4, 7,
                                             Alu.logical_shift_right, Alu.bitwise_and)
                        vector.tensor_scalar(ss2, dsrc, 4, 8,
                                             Alu.logical_shift_right, Alu.bitwise_and)
                    t1 = dt1[:, 0:256]
                    u_ = du[:, 0:256]
                    w_ = dw_[:, 0:256]
                    qv = dq[:, 0:256]
                    vector.tensor_scalar(t1, kk, 0.5, None, Alu.mult)
                    vector.tensor_scalar(u_, t1, 2.0, -2.0, Alu.mult, Alu.add)
                    vector.tensor_scalar(w_, t1, 4.0, -8.0, Alu.mult, Alu.add)
                    vector.tensor_tensor(u_, t1, u_, Alu.max)
                    vector.tensor_tensor(u_, u_, w_, Alu.max)
                    vector.tensor_scalar(w_, ss2, -0.25, 1.0, Alu.mult, Alu.add)
                    vector.tensor_tensor(qv, u_, w_, Alu.mult)
                    vector.tensor_tensor(dwout[h][:, qs], qv,
                                         dbc[i % 2][:, qs], Alu.mult).inc("v")
                    if s4 == 3:
                        B.note(f"dvex{i}", "v")
                        if sl["kind"] == "gu" or sl["s"] == 1:
                            B.note(f"dv_stage{sl['stage']}", "v")
                        dq_seg[0] = 0
                        dq_i[0] += 1
                    else:
                        dq_seg[0] = s4 + 1

            vector.memset(ones_k[:, :], 1.0)
            vector.memset(ones_1[:, :], 1.0)
            vector.memset(zeros_t[:, :], 0.0)
            vector.memset(S_sb[:, :], 0.0).inc("v")
            B.note("v_init", "v")
            for kc in range(KC):
                vector.wait("sx", f"sx{kc+1}" if kc + 1 < KC else "sx_all")
                if kc >= 1:
                    vector.wait("mm", f"lnmm{kc-1}")
                vector.tensor_copy(xbf[0][:, :], xch(kc)).inc("v")
                B.note(f"lncp{kc}", "v")
            vector.wait("sx", "sx_all")
            vector.tensor_tensor(bds_sb[:, :], bd_sb[:, :], db_sb[:, :], Alu.add).inc("v")
            B.note("v_bds", "v")
            vector.wait("mm", "mm_stats")
            for th in range(2):
                slc = slice(th * TH, (th + 1) * TH)
                vector.tensor_scalar(ta_s[:, slc], psS4[th][0:1, :], 1.0 / D, None, Alu.mult)
                vector.tensor_scalar(vs_row[:, slc], psS4[2 + th][0:1, :], 1.0 / D, EPS,
                                     Alu.mult, Alu.add)
            for th in range(2):
                slc = slice(th * TH, (th + 1) * TH)
                vector.tensor_tensor(sq_r[th], ta_s[:, slc], ta_s[:, slc], Alu.mult)
                vector.tensor_tensor(var_r[th], vs_row[:, slc], sq_r[th], Alu.subtract)
            ch = None
            for th in range(2):
                ch = vector.reciprocal(sq_r[th], var_r[th])
            ch.inc("v")
            B.note("v_lnstat", "v")
            vector.wait("se", "se_rstd")
            vector.tensor_tensor(mr_bf[:, :], ta_s[:, :], rstd_bf[:, :], Alu.mult).inc("v")
            B.note("v_mr", "v")
            vector.wait("se", "se_bc")
            for kc in range(KC):
                vector.tensor_tensor(em_t[:, :], xch(kc), rstd_b[:, :], Alu.mult)
                vector.tensor_tensor(xnch(kc), em_t[:, :], mrr_b[:, :], Alu.subtract).inc("v")
                B.note(f"xn{kc}", "v")
            B.note("v_xn", "v")
            for ec in range(KC):
                par = ec % 2
                vector.wait("se", f"ecse{ec}")
                if ec >= 2:
                    vector.wait("slp", f"lspill{ec-2}")
                vector.tensor_tensor(em_t[:, :], dec_t[par][:, :], inp_t[par][:, :], Alu.mult)
                vector.tensor_tensor(em_t[:, :], inp_t[par][:, :], em_t[:, :], Alu.subtract)
                vector.tensor_tensor_scan(Lt[par][:, :], dec_t[par][:, :], em_t[:, :], 0.0,
                                          Alu.mult, Alu.add)
                vector.tensor_tensor_scan(Pt[par][:, :], dec_t[par][:, :], zeros_t[:, :], 1.0,
                                          Alu.mult, Alu.add).inc("v")
                B.note(f"scandone{ec}", "v")
                vector.wait("v", f"scandone{ec}")
                vector.tensor_copy(ex_sb[:, ec:ec + 1], Pt[par][:, T - 1:T])
                vector.tensor_copy(ex_sb[:, KC + ec:KC + ec + 1],
                                   Lt[par][:, T - 1:T]).inc("v")
                B.note(f"ecv{ec}", "v")
                deq_seg(3 if ec < 8 else 2)
            B.note("scans_done", "v")
            vector.wait("sexg", "exall")
            for j in range(NCORES - 1):
                A_j = exa_sb[:, 32 * j:32 * j + 16]
                B_j = exa_sb[:, 32 * j + 16:32 * j + 32]
                vector.tensor_tensor(Tt_sb[:, :], A_j, S_sb[:, :], Alu.mult)
                vector.tensor_tensor(Tt_sb[:, :], Tt_sb[:, :], B_j, Alu.add)
                vector.tensor_tensor(Dt_sb[:, :], Tt_sb[:, :], S_sb[:, :], Alu.subtract)
                vector.tensor_scalar(Dt_sb[:, :], Dt_sb[:, :], msk_sb[:, j:j + 1],
                                     None, Alu.mult)
                ch = vector.tensor_tensor(S_sb[:, :], S_sb[:, :], Dt_sb[:, :], Alu.add)
            ch.inc("v")
            B.note("v_prefix", "v")
            for ec in range(KC):
                par = ec % 2
                vector.wait("se", f"ecgse{ec}")
                vector.wait("slpr", f"lpre{min(ec + 1, KC - 1)}")
                if ec == KC - 1:
                    vector.wait("swa", "wload49")
                vector.scalar_tensor_tensor(em_t[:, :], Pt[par][:, :], S_sb[:, ec:ec + 1],
                                            Lt[par][:, :], Alu.mult, Alu.add)
                vector.tensor_tensor(ych(ec), gtmp[par][:, :], em_t[:, :], Alu.mult).inc("v")
                B.note(f"ecy{ec}", "v")
                deq_seg(2)
            B.note("v_y", "v")
            for dc in range(KC):
                vector.wait("mm", f"dcmm{dc}")
                for th in range(2):
                    xsl = x_sb[:, dc * T + th * TH: dc * T + th * TH + TH]
                    vector.scalar_tensor_tensor(xsl, psO4[dc % 2][th][:, :],
                                                bo_sb[:, dc:dc + 1],
                                                xsl, Alu.add, Alu.add).inc("v")
                B.note(f"dcv{dc}", "v")
                deq_seg(1)
            B.note("v_attn_done", "v")

            for g in range(NGRP):
                for u in range(16):
                    idx = g * 16 + u
                    deq_seg(1)
                    vector.wait("se", f"gel{idx}")
                    vector.wait("mm", f"guU{idx}")
                    nc8, th = u // 2, u % 2
                    ioff = nc8 * T + th * TH
                    vector.tensor_tensor(inter[:, ioff:ioff + TH], gel_t[idx % 2][:, :],
                                         psU[idx % 2][:, :], Alu.mult).inc("v")
                    B.note(f"intr{idx}", "v")
                for dc in range(KC):
                    vector.wait("mm", f"dn{g}_{dc}")
                    ch = None
                    for th in range(2):
                        xsl = x_sb[:, dc * T + th * TH: dc * T + th * TH + TH]
                        ch = vector.tensor_tensor(xsl, xsl, psDn4[dc % 2][th][:, :],
                                                  Alu.add)
                    ch.inc("v")
                    B.note(f"xadd{g}_{dc}", "v")
                    if dc % 2 == 1:
                        deq_seg(1)
                B.note(f"gxadd{g}", "v")
            deq_seg(4 * NSL)
            B.note("ffn_done", "v")

            for kc in range(KC):
                scalar.wait("sx", f"sx{kc+1}" if kc + 1 < KC else "sx_all")
                if kc >= 1:
                    scalar.wait("mm", f"lnmm{kc-1}")
                scalar.activation(xsq[0][:, :], xch(kc), Act.Square).inc("se")
                B.note(f"lnsq{kc}", "se")
            scalar.wait("v", "v_lnstat")
            ch = None
            for th in range(2):
                ch = scalar.activation(rstd_bf[:, th * TH:(th + 1) * TH],
                                       sq_r[th], Act.Sqrt)
            ch.inc("se")
            B.note("se_rstd", "se")
            for th in range(2):
                slc = slice(th * TH, (th + 1) * TH)
                scalar.wait("mm", f"mmbc{th}")
                scalar.copy(rstd_b[:, slc], psBC[0][:, :])
                scalar.copy(mrr_b[:, slc], psBC[1][:, :]).inc("se", 2)
                B.note(f"se_bct{th}", "se")
            B.note("se_bc", "se")
            wconv = [0]

            def conv_steps(upto):
                while wconv[0] < min(upto, NWS):
                    i = wconv[0]
                    scalar.wait("swa", f"wload{i+1}" if i + 1 < NWS else "wl_tail")
                    if i >= 4:
                        scalar.wait("mm", f"wuse{i-4}")
                    scalar.copy(wbf[i % 4][:, :], wst[i % 2][:, :]).inc("se")
                    B.note(f"wconv{i}", "se")
                    wconv[0] += 1

            conv_steps(2)
            for ec in range(KC):
                par = ec % 2
                scalar.wait("mm", f"ecmmi{ec}")
                if ec >= 2:
                    scalar.wait("v", f"ecv{ec-2}")
                for th in range(2):
                    scalar.activation(inp_t[par][:, th * TH:(th + 1) * TH],
                                      psI4[par][th][:, :],
                                      Act.Silu, bias=bin_sb[:, ec:ec + 1], scale=1.0)
                scalar.wait("mm", f"ecmmd{ec}")
                scalar.wait("v", "v_bds")
                ch = None
                for th in range(2):
                    ch = scalar.activation(dec_t[par][:, th * TH:(th + 1) * TH],
                                           psDc4[par][th][:, :],
                                           Act.Sigmoid, bias=bds_sb[:, ec:ec + 1], scale=1.0)
                ch.inc("se")
                B.note(f"ecse{ec}", "se")
                if ec >= 1:
                    scalar.wait("v", f"ecv{ec-1}")
                    scalar.dma_start(out=lp_dram[1, ec - 1], in_=Lt[(ec - 1) % 2][:, :]).dma("slp")
                    scalar.dma_start(out=lp_dram[0, ec - 1], in_=Pt[(ec - 1) % 2][:, :]).dma("slp")
                    B.note(f"lspill{ec-1}", "slp")
                conv_steps(min(2 * ec + 4, 34))
            scalar.wait("v", f"ecv{KC-1}")
            scalar.dma_start(out=lp_dram[1, KC - 1], in_=Lt[(KC - 1) % 2][:, :]).dma("slp")
            scalar.dma_start(out=lp_dram[0, KC - 1], in_=Pt[(KC - 1) % 2][:, :]).dma("slp")
            B.note(f"lspill{KC-1}", "slp")
            B.note("lspill_all", "slp")
            for ec in range(KC):
                conv_steps(min(2 * KC + ec + 3, 47))
                scalar.wait("mm", f"ecgmm{ec}")
                if ec >= 2:
                    scalar.wait("v", f"ecy{ec-2}")
                ch = None
                for th in range(2):
                    ch = scalar.activation(gtmp[ec % 2][:, th * TH:(th + 1) * TH],
                                           psI4[ec % 2][th][:, :],
                                           Act.Sigmoid, bias=bg_sb[:, ec:ec + 1], scale=1.0)
                ch.inc("se")
                B.note(f"ecgse{ec}", "se")
                conv_steps(min(2 * KC + ec + 4, 48 if ec >= 13 else 47))
            conv_steps(NWS)
            for g in range(NGRP):
                if g == 0:
                    scalar.wait("v", "v_attn_done")
                for idx in range(g * 16, g * 16 + 16):
                    scalar.wait("mm", f"gu{idx}")
                    if idx >= 2:
                        scalar.wait("v", f"intr{idx-2}")
                    scalar.activation(gel_t[idx % 2][:, :], psG[idx % 2][:, :],
                                      Act.Gelu_apprx_tanh).inc("se")
                    B.note(f"gel{idx}", "se")
                    if idx == g * 16 + 1:
                        scalar.wait("scc", f"cc{g}")
                        if g >= 1:
                            scalar.wait("mm", f"duse{g-1}")
                        for j8 in range(8):
                            qq, e = j8 // 2, j8 % 2
                            for half in range(2):
                                rank = 2 * qq + half
                                scalar.dma_start(
                                    out=Wdb[half * 64:(half + 1) * 64,
                                            j8 * D:(j8 + 1) * D],
                                    in_=agpan(g, rank, 4 + e)
                                        .rearrange("(h r) dl -> h r dl", h=2)
                                        .transpose([1, 0, 2])).dma("slp")
                        scalar.dma_start(out=fence_sb[:, :], in_=bin_d[:, 0:4]).dma("slp")
                        B.note(f"fwdf{g}", "slp")

            for kc in range(KC):
                tensor.wait("v", f"lncp{kc}")
                tensor.wait("se", f"lnsq{kc}")
                ch = None
                for th in range(2):
                    tensor.matmul(psS4[th][0:1, :], ones_k[:, :],
                                  xbf[0][:, th * TH:(th + 1) * TH],
                                  start=(kc == 0), stop=(kc == KC - 1))
                    ch = tensor.matmul(psS4[2 + th][0:1, :], ones_k[:, :],
                                       xsq[0][:, th * TH:(th + 1) * TH],
                                       start=(kc == 0), stop=(kc == KC - 1))
                ch.inc("mm")
                B.note(f"lnmm{kc}", "mm")
            B.note("mm_stats", "mm")
            tensor.wait("v", "v_mr")
            for th in range(2):
                slc = slice(th * TH, (th + 1) * TH)
                if th == 1:
                    tensor.wait("se", "se_bct0")
                tensor.matmul(psBC[0][:, :], ones_1[:, :], rstd_bf[:, slc],
                              start=True, stop=True)
                tensor.matmul(psBC[1][:, :], ones_1[:, :], mr_bf[:, slc],
                              start=True, stop=True).inc("mm")
                B.note(f"mmbc{th}", "mm")
            for ec in range(KC):
                ci, cd = 2 * ec, 2 * ec + 1
                par = ec % 2
                if ec == 0:
                    tensor.wait("se", "se_bc")
                else:
                    tensor.wait("v", "v_xn")
                tensor.wait("se", f"wconv{ci}")
                if ec >= 2:
                    tensor.wait("se", f"ecse{ec-2}")
                ch = None
                for kc in range(KC):
                    if ec == 0:
                        tensor.wait("v", f"xn{kc}")
                    for th in range(2):
                        ch = tensor.matmul(psI4[par][th][:, :],
                                           wbf[ci % 4][:, kc * P:(kc + 1) * P],
                                           xnch(kc)[:, th * TH:(th + 1) * TH],
                                           start=(kc == 0), stop=(kc == KC - 1))
                ch.inc("mm")
                B.note(f"ecmmi{ec}", "mm")
                B.note(f"wuse{ci}", "mm")
                tensor.wait("se", f"wconv{cd}")
                if ec == 0:
                    tensor.wait("v", "v_xn")
                ch = None
                for kc in range(KC):
                    for th in range(2):
                        ch = tensor.matmul(psDc4[par][th][:, :],
                                           wbf[cd % 4][:, kc * P:(kc + 1) * P],
                                           xnch(kc)[:, th * TH:(th + 1) * TH],
                                           start=(kc == 0), stop=(kc == KC - 1))
                ch.inc("mm")
                B.note(f"ecmmd{ec}", "mm")
                B.note(f"wuse{cd}", "mm")
            for ec in range(KC):
                cg = 2 * KC + ec
                tensor.wait("se", f"wconv{cg}")
                if ec >= 2:
                    tensor.wait("se", f"ecgse{ec-2}")
                else:
                    tensor.wait("se", f"ecse{KC-1}")
                ch = None
                for kc in range(KC):
                    for th in range(2):
                        ch = tensor.matmul(psI4[ec % 2][th][:, :],
                                           wbf[cg % 4][:, kc * P:(kc + 1) * P],
                                           xnch(kc)[:, th * TH:(th + 1) * TH],
                                           start=(kc == 0), stop=(kc == KC - 1))
                ch.inc("mm")
                B.note(f"ecgmm{ec}", "mm")
                B.note(f"wuse{cg}", "mm")
            for dc in range(KC):
                co = 3 * KC + dc
                tensor.wait("se", f"wconv{co}")
                tensor.wait("v", "v_y")
                if dc >= 2:
                    tensor.wait("v", f"dcv{dc-2}")
                ch = None
                for ec in range(KC):
                    for th in range(2):
                        ch = tensor.matmul(psO4[dc % 2][th][:, :],
                                           wbf[co % 4][:, ec * P:(ec + 1) * P],
                                           ych(ec)[:, th * TH:(th + 1) * TH],
                                           start=(ec == 0), stop=(ec == KC - 1))
                ch.inc("mm")
                B.note(f"dcmm{dc}", "mm")
                B.note(f"wuse{co}", "mm")
            for g in range(NGRP):
                for nc8 in range(8):
                    nc4, h128 = nc8 // 2, nc8 % 2
                    slot_g = (g * 8 + 2 * nc4) % NWR
                    slot_u = (g * 8 + 2 * nc4 + 1) % NWR
                    idx0 = g * 16 + nc8 * 2
                    idx1 = idx0 + 1
                    p0, p1 = idx0 % 2, idx1 % 2
                    if h128 == 0:
                        key = (f"fgl{g*8 + 2*nc4 + 2}" if nc4 < 3
                               else f"fguf{g}")
                        tensor.wait("sbd", key)
                    if idx0 >= 2:
                        tensor.wait("se", f"gel{idx0-2}")
                        tensor.wait("se", f"gel{idx1-2}")
                    else:
                        tensor.wait("v", "v_attn_done")
                    coff = h128 * P
                    ch = None
                    for kc in range(KC):
                        for th, pp in ((0, p0), (1, p1)):
                            ch = tensor.matmul(psG[pp][:, :],
                                               wrng[slot_g][:, kc * 256 + coff:
                                                            kc * 256 + coff + P],
                                               xnch(kc)[:, th * TH:(th + 1) * TH],
                                               start=(kc == 0), stop=(kc == KC - 1))
                    ch.inc("mm")
                    B.note(f"gu{idx0}", "mm")
                    B.note(f"gu{idx1}", "mm")
                    if idx0 >= 2:
                        tensor.wait("v", f"intr{idx0-2}")
                        tensor.wait("v", f"intr{idx1-2}")
                    ch = None
                    for kc in range(KC):
                        for th, pp in ((0, p0), (1, p1)):
                            ch = tensor.matmul(psU[pp][:, :],
                                               wrng[slot_u][:, kc * 256 + coff:
                                                            kc * 256 + coff + P],
                                               xnch(kc)[:, th * TH:(th + 1) * TH],
                                               start=(kc == 0), stop=(kc == KC - 1))
                    ch.inc("mm")
                    B.note(f"guU{idx0}", "mm")
                    B.note(f"guU{idx1}", "mm")
                tensor.wait("v", f"intr{g*16 + 15}")
                tensor.wait("slp", f"fwdf{g}")
                for dc in range(KC):
                    if dc >= 2:
                        tensor.wait("v", f"xadd{g}_{dc-2}")
                    elif g >= 1:
                        tensor.wait("v", f"xadd{g-1}_{14+dc}")
                    ch = None
                    for j8 in range(8):
                        for th in range(2):
                            ch = tensor.matmul(psDn4[dc % 2][th][:, :],
                                               Wdb[:, j8 * D + dc * P: j8 * D + dc * P + P],
                                               inter[:, j8 * T + th * TH: j8 * T + th * TH + TH],
                                               start=(j8 == 0), stop=(j8 == 7))
                    ch.inc("mm")
                    B.note(f"dn{g}_{dc}", "mm")
                B.note(f"duse{g}", "mm")

        rec = {}
        B1 = _Builder(False, rec)
        emit(B1)
        B2 = _Builder(True, rec)
        emit(B2)

        def flush(engname):
            stream = B2.stream[engname]

            def body(eng):
                for it in stream:
                    if it[0] == "w":
                        eng.wait_ge(sems[it[1]], it[2])
                    else:
                        _, name, args, kwargs, ri = it
                        instr = getattr(eng, name)(*args, **kwargs)
                        for s, k in ri.incs:
                            instr.then_inc(sems[s], k)
            return body

        block.sync(flush("sync"))
        block.vector(flush("vector"))
        block.scalar(flush("scalar"))
        block.tensor(flush("tensor"))
        block.gpsimd(flush("gpsimd"))

    return nc


_PROGRAM_CACHE = {}


def _get_program():
    if "nc" not in _PROGRAM_CACHE:
        _PROGRAM_CACHE["nc"] = build_program()
    return _PROGRAM_CACHE["nc"]


def _perm_interleave(n):
    out = np.empty(n, np.int64)
    c = np.arange(n // 256)[:, None, None]
    e = np.arange(2)[None, :, None]
    p = np.arange(P)[None, None, :]
    idx = 128 * (2 * c + e) + p
    val = 256 * c + 2 * p + e
    out[idx.ravel()] = val.ravel()
    return out


def kernel(**inputs):
    SIG = _perm_interleave(D)
    TAU = _perm_interleave(NMLP)

    x = np.asarray(inputs["L0_attn_x"])[0]
    W_in = np.asarray(inputs["L0_attn_W_in"])
    W_dec = np.asarray(inputs["L0_attn_W_decay"])
    W_gate = np.asarray(inputs["L0_attn_W_gate"])
    W_out = np.asarray(inputs["L0_attn_W_out"])

    def tile_kx(W):
        return np.ascontiguousarray(
            W.reshape(KC, P, KC, P).transpose(2, 1, 0, 3)).astype(np.float32)

    wti_h = tile_kx(W_in[SIG])
    wtd_h = tile_kx(W_dec[SIG])
    wtg_h = tile_kx(W_gate[SIG])
    wto_h = tile_kx(W_out[:, SIG])

    tb = lambda b: np.ascontiguousarray(np.asarray(b).reshape(KC, P).T).astype(np.float32)
    bin_h = tb(inputs["L0_attn_b_in"])
    bg_h = tb(inputs["L0_attn_b_gate"])
    bd_h = tb(inputs["L0_attn_b_decay"])
    db_h = tb(inputs["L0_attn_decay_bias"])
    bo_h = tb(np.asarray(inputs["L0_attn_b_out"])[SIG])

    u8v = lambda a, n, kh: np.asarray(a).view(np.uint8).reshape(n, kh, 4)[:, :, 0]
    wg_u8 = u8v(inputs["L0_ffn_gate_W_packed"], NMLP, 1024)
    wu_u8 = u8v(inputs["L0_ffn_up_W_packed"], NMLP, 1024)
    wd_u8 = u8v(inputs["L0_ffn_down_W_packed"], D, 8192)
    WgT = np.ascontiguousarray(wg_u8[TAU].T)
    WuT = np.ascontiguousarray(wu_u8[TAU].T)
    WdT = np.ascontiguousarray(wd_u8[SIG].T)
    gsT = np.ascontiguousarray(np.asarray(inputs["L0_ffn_gate_scales"])[TAU].T).astype(np.float32)
    usT = np.ascontiguousarray(np.asarray(inputs["L0_ffn_up_scales"])[TAU].T).astype(np.float32)
    dsT = np.ascontiguousarray(np.asarray(inputs["L0_ffn_down_scales"])[SIG].T).astype(np.float32)

    xT = np.ascontiguousarray(x.T[SIG])

    in_maps = []
    for c in range(NCORES):
        rows = np.concatenate([np.arange(512 * g + 64 * c, 512 * g + 64 * c + 64)
                               for g in range(NGRP)])
        wds_h = np.ascontiguousarray(WdT[rows].reshape(NGRP, 64, D))
        dsrows = np.concatenate([np.arange(32 * g + 4 * c, 32 * g + 4 * c + 4)
                                 for g in range(NGRP)])
        ds_h = np.ascontiguousarray(dsT[dsrows].reshape(NGRP, 4, D))
        msk_h = np.tile((np.arange(8) < c).astype(np.float32), (P, 1))
        in_maps.append(dict(
            xin=np.ascontiguousarray(xT[:, c * T:(c + 1) * T]),
            wti=wti_h, wtd=wtd_h, wtg=wtg_h, wto=wto_h,
            bin_d=bin_h, bg_d=bg_h, bd_d=bd_h, db_d=db_h, bo_d=bo_h,
            wgs=np.ascontiguousarray(WgT[128 * c:128 * (c + 1)]),
            wus=np.ascontiguousarray(WuT[128 * c:128 * (c + 1)]),
            wds=wds_h,
            gs_d=np.ascontiguousarray(gsT[8 * c:8 * c + 8]),
            us_d=np.ascontiguousarray(usT[8 * c:8 * c + 8]),
            ds_d=ds_h,
            msk_d=np.ascontiguousarray(msk_h),
        ))

    nc = _get_program()
    res = run_bass_kernel_spmd(nc, in_maps, core_ids=list(range(NCORES)))

    out_dev = np.concatenate([np.asarray(res.results[c]["out_d"]) for c in range(NCORES)],
                             axis=1)
    out = np.empty_like(out_dev)
    out[SIG] = out_dev
    return np.ascontiguousarray(out.T)[None].astype(np.float32)


# version 18
# speedup vs baseline: 1.0212x; 1.0212x over previous
import numpy as np
import contextlib
import concourse.bass as bass
import concourse.mybir as mybir
from concourse.bass_utils import run_bass_kernel_spmd

F32 = mybir.dt.float32
BF16 = mybir.dt.bfloat16
U8 = mybir.dt.uint8
Alu = mybir.AluOpType
Act = mybir.ActivationFunctionType

P = 128
D = 2048
KC = 16
T = 1024
TH = 512
NCORES = 8
NMLP = 16384
NGRP = 16
EPS = 1e-5

SEM_NAMES = ["sx", "swa", "spk", "sdq", "slp", "slpr", "sexg", "scc", "sbd",
             "sout", "v", "se", "mm", "gp"]
ENGS = ["sync", "vector", "scalar", "tensor", "gpsimd"]


class _RecInstr:
    __slots__ = ("incs",)

    def __init__(self):
        self.incs = []

    def then_inc(self, sem_name, k):
        self.incs.append((sem_name, k))
        return self


class _EngProxy:
    def __init__(self, B, eng):
        self.B = B
        self.eng = eng

    def wait(self, sem, key):
        val = key if isinstance(key, int) else self.B.rec_get(key)
        if self.B.emitting and val > 0:
            self.B.stream[self.eng].append(("w", sem, val))

    def op(self, name, *args, **kwargs):
        ri = _RecInstr()
        if self.B.emitting:
            self.B.stream[self.eng].append(("o", name, args, kwargs, ri))
        return _Chain(self.B, ri)

    def __getattr__(self, name):
        return lambda *a, **k: self.op(name, *a, **k)


class _Chain:
    __slots__ = ("B", "ri")

    def __init__(self, B, ri):
        self.B = B
        self.ri = ri

    def inc(self, sem, k=1):
        self.ri.then_inc(sem, k)
        self.B.cnt[sem] += k
        return self.B.cnt[sem]

    def dma(self, sem):
        return self.inc(sem, 16)


class _Builder:
    def __init__(self, emitting, rec):
        self.emitting = emitting
        self.rec = rec
        self.cnt = {s: 0 for s in SEM_NAMES}
        self.stream = {e: [] for e in ENGS}
        self.proxies = {e: _EngProxy(self, e) for e in ENGS}

    def __getattr__(self, name):
        if name in ENGS:
            return self.proxies[name]
        raise AttributeError(name)

    def note(self, key, sem):
        if not self.emitting:
            self.rec[key] = self.cnt[sem]
        else:
            assert self.rec.get(key) == self.cnt[sem], \
                f"pass mismatch at {key}: {self.rec.get(key)} != {self.cnt[sem]}"

    def rec_get(self, key):
        if not self.emitting:
            return 0
        assert key in self.rec, f"missing note {key}"
        return self.rec[key]

    def n(self, sem):
        return self.cnt[sem]


def build_program():
    nc = bass.Bass("TRN2", target_bir_lowering=False, debug=False,
                   detect_race_conditions=False)

    xin = nc.dram_tensor("xin", [D, T], F32, kind="ExternalInput")
    wti = nc.dram_tensor("wti", [KC, P, KC, P], F32, kind="ExternalInput")
    wtd = nc.dram_tensor("wtd", [KC, P, KC, P], F32, kind="ExternalInput")
    wtg = nc.dram_tensor("wtg", [KC, P, KC, P], F32, kind="ExternalInput")
    wto = nc.dram_tensor("wto", [KC, P, KC, P], F32, kind="ExternalInput")
    bin_d = nc.dram_tensor("bin_d", [P, KC], F32, kind="ExternalInput")
    bg_d = nc.dram_tensor("bg_d", [P, KC], F32, kind="ExternalInput")
    bd_d = nc.dram_tensor("bd_d", [P, KC], F32, kind="ExternalInput")
    db_d = nc.dram_tensor("db_d", [P, KC], F32, kind="ExternalInput")
    bo_d = nc.dram_tensor("bo_d", [P, KC], F32, kind="ExternalInput")
    wgs = nc.dram_tensor("wgs", [P, NMLP], U8, kind="ExternalInput")
    wus = nc.dram_tensor("wus", [P, NMLP], U8, kind="ExternalInput")
    wds = nc.dram_tensor("wds", [NGRP, 64, D], U8, kind="ExternalInput")
    gs_d = nc.dram_tensor("gs_d", [8, NMLP], F32, kind="ExternalInput")
    us_d = nc.dram_tensor("us_d", [8, NMLP], F32, kind="ExternalInput")
    ds_d = nc.dram_tensor("ds_d", [NGRP, 4, D], F32, kind="ExternalInput")
    msk_d = nc.dram_tensor("msk_d", [P, 8], F32, kind="ExternalInput")
    out_d = nc.dram_tensor("out_d", [D, T], F32, kind="ExternalOutput")

    lp_dram = nc.dram_tensor("lp_dram", [2, KC, P, T], BF16)
    deq_in = [nc.dram_tensor(f"deq_in{g}", [6, P, 1024], BF16)
              for g in range(NGRP)]
    ag_out = [nc.dram_tensor(f"ag_out{g}", [NCORES, 6, P, 1024], BF16,
                             addr_space="Shared") for g in range(NGRP)]
    ex_in = nc.dram_tensor("ex_in", [P, 32], F32)
    ex_out = nc.dram_tensor("ex_out", [NCORES, P, 32], F32, addr_space="Shared")

    def dpan(g, p):
        return deq_in[g][p]

    def agpan(g, rank, p):
        return ag_out[g][rank, p]

    est = contextlib.ExitStack()
    with est:
        sbt = lambda name, shape, dt: est.enter_context(nc.sbuf_tensor(name, shape, dt))
        x_sb = sbt("x_sb", [P, KC * T], F32)
        xn_sb = sbt("xn_sb", [P, KC * T], BF16)
        bin_sb = sbt("bin_sb", [P, KC], F32)
        bg_sb = sbt("bg_sb", [P, KC], F32)
        bd_sb = sbt("bd_sb", [P, KC], F32)
        db_sb = sbt("db_sb", [P, KC], F32)
        bo_sb = sbt("bo_sb", [P, KC], F32)
        bds_sb = sbt("bds_sb", [P, KC], F32)
        msk_sb = sbt("msk_sb", [P, 8], F32)
        ex_sb = sbt("ex_sb", [P, 32], F32)
        exa_sb = sbt("exa_sb", [P, 256], F32)
        S_sb = sbt("S_sb", [P, 16], F32)
        Tt_sb = sbt("Tt_sb", [P, 16], F32)
        Dt_sb = sbt("Dt_sb", [P, 16], F32)
        ones_k = sbt("ones_k", [P, 1], BF16)
        ones_1 = sbt("ones_1", [1, P], BF16)
        zeros_t = sbt("zeros_t", [P, T], BF16)
        em_t = sbt("em_t", [P, T], F32)
        fence_sb = sbt("fence_sb", [P, 4], F32)
        dstage = [sbt(f"dstage{i}", [P, 1024], U8) for i in range(2)]
        dbc = [sbt(f"dbc{i}", [P, TH], F32) for i in range(2)]
        dklo = sbt("dklo", [P, TH], U8)
        dslo = sbt("dslo", [P, TH], U8)
        dt1 = sbt("dt1", [P, TH], BF16)
        du = sbt("du", [P, TH], BF16)
        dw_ = sbt("dw_", [P, TH], BF16)
        dq = sbt("dq", [P, TH], BF16)
        dwout = [sbt(f"dwout{i}", [P, TH], BF16) for i in range(2)]
        attn_sc = contextlib.ExitStack()
        asbt = lambda name, shape, dt: attn_sc.enter_context(nc.sbuf_tensor(name, shape, dt))
        y_sb = asbt("y_sb", [P, KC * T], BF16)
        wst = [asbt(f"wst{i}", [P, KC * P], F32) for i in range(2)]
        wbf = [asbt(f"wbf{i}", [P, KC * P], BF16) for i in range(3)]
        inp_t = [asbt(f"inp_t{i}", [P, T], F32) for i in range(2)]
        dec_t = [asbt(f"dec_t{i}", [P, T], F32) for i in range(2)]
        Lt = [asbt(f"Lt{i}", [P, T], BF16) for i in range(2)]
        Pt = [asbt(f"Pt{i}", [P, T], BF16) for i in range(2)]
        gtmp = [asbt(f"gtmp{i}", [P, T], BF16) for i in range(2)]
        xbf = [y_sb[:, 0:T]]
        xsq = [y_sb[:, T:2 * T]]
        rstd_b = y_sb[:, 2 * T:3 * T]
        mrr_b = y_sb[:, 3 * T:4 * T]
        rstd_bf = y_sb[0:1, 4 * T:5 * T]
        mr_bf = y_sb[0:1, 5 * T:6 * T]
        ta_s = em_t[0:1, :]
        attn_sc.close()
        NWR = 5
        wrng = [sbt(f"wrng{i}", [P, KC * 256], BF16) for i in range(NWR)]
        Wdb = sbt("Wdb", [P, 8 * D], BF16)
        inter = sbt("inter", [P, 8 * T], BF16)
        gel_t = [sbt(f"gel_t{i}", [P, TH], BF16) for i in range(2)]

        psA = est.enter_context(nc.psum_tensor("psA", [P, TH], F32))
        psB = est.enter_context(nc.psum_tensor("psB", [P, TH], F32))
        psC = est.enter_context(nc.psum_tensor("psC", [P, TH], F32))
        psD2 = est.enter_context(nc.psum_tensor("psD2", [P, TH], F32))
        psE = est.enter_context(nc.psum_tensor("psE", [P, TH], F32))
        psF = est.enter_context(nc.psum_tensor("psF", [P, TH], F32))
        psGb = est.enter_context(nc.psum_tensor("psGb", [P, TH], F32))
        psHb = est.enter_context(nc.psum_tensor("psHb", [P, TH], F32))
        psI4 = [[psA, psB], [psGb, psHb]]
        psDc4 = [[psC, psD2], [psE, psF]]
        psO4 = [[psE, psF], [psC, psD2]]
        psG = [psA, psB]
        psU = [psC, psD2]
        psDn4 = [[psE, psF], [psGb, psHb]]
        psS4 = [psGb, psHb, psE, psF]
        psBC = [psA, psB]

        sems = {name: est.enter_context(nc.semaphore(name)) for name in SEM_NAMES}
        block = est.enter_context(nc.Block())

        xch = lambda kc: x_sb[:, kc * T:(kc + 1) * T]
        xnch = lambda kc: xn_sb[:, kc * T:(kc + 1) * T]
        ych = lambda kc: y_sb[:, kc * T:(kc + 1) * T]

        deq_slices = []
        sid = 0
        for g in range(NGRP):
            for w, (wsrc, pbase) in enumerate((("wg", 0), ("wu", 2))):
                for s in range(2):
                    deq_slices.append(dict(kind="gu", w=wsrc, g=g, s=s, pbase=pbase,
                                           stage=sid, fresh=True))
                    sid += 1
            for s in range(2):
                deq_slices.append(dict(kind="wd", w="wd", g=g, s=s, pbase=4,
                                       stage=sid if s == 0 else sid - 1, fresh=(s == 0)))
                if s == 0:
                    sid += 1
        NSL = len(deq_slices)

        wsteps = []
        for ec in range(KC):
            wsteps.append(("wti", ec))
            wsteps.append(("wtd", ec))
        for ec in range(KC):
            wsteps.append(("wtg", ec))
        for dc in range(KC):
            wsteps.append(("wto", dc))
        wt_map = {"wti": wti, "wtd": wtd, "wtg": wtg, "wto": wto}
        NWS = len(wsteps)

        def emit(B):
            sync, vector, scalar, tensor, gpsimd = (B.sync, B.vector, B.scalar,
                                                    B.tensor, B.gpsimd)
            vs_row = inp_t[0][0:1, :]
            sq_r = [psGb[0:1, :], psHb[0:1, :]]
            var_r = [psC[0:1, :], psD2[0:1, :]]

            for kc in range(KC):
                sync.dma_start(out=x_sb[:, kc * T:(kc + 1) * T],
                               in_=xin[kc * P:(kc + 1) * P, :]).dma("sx")
                B.note(f"sx{kc}", "sx")
            for dst, src in ((bin_sb, bin_d), (bg_sb, bg_d), (bd_sb, bd_d),
                             (db_sb, db_d), (bo_sb, bo_d), (msk_sb, msk_d)):
                sync.dma_start(out=dst[:, :], in_=src[:, :]).dma("sx")
            B.note("sx_all", "sx")

            def wstep_load(i):
                wname, ec = wsteps[i]
                if i >= 2:
                    sync.wait("se", f"wconv{i-2}")
                sync.dma_start(out=wst[i % 2][:, :], in_=wt_map[wname][ec]).dma("swa")
                B.note(f"wload{i}", "swa")

            def lp_reload(r):
                sync.wait("slp", "lspill_all")
                if r >= 2:
                    sync.wait("v", f"ecy{r-2}")
                sync.dma_start(out=Pt[r % 2][:, :], in_=lp_dram[0, r]).dma("slpr")
                sync.dma_start(out=Lt[r % 2][:, :], in_=lp_dram[1, r]).dma("slpr")
                B.note(f"lpre{r}", "slpr")

            for i in range(2 * KC):
                wstep_load(i)
            for i in range(2 * KC, 3 * KC):
                wstep_load(i)
                if i - 34 >= 0:
                    lp_reload(i - 34)
            lp_reload(KC - 2)
            lp_reload(KC - 1)
            for i in range(3 * KC, NWS):
                wstep_load(i)
            sync.dma_start(out=fence_sb[:, :], in_=bin_d[:, 0:4]).dma("swa")
            B.note("wl_tail", "swa")

            sync.wait("v", "v_attn_done")
            for g in range(NGRP):
                sync.wait("scc", f"cc{g}")
                pan = ag_out[g].ap().rearrange("r s q t -> r (s q) t")
                for nc4 in range(4):
                    for mat, pb in ((0, 0), (1, 2)):
                        jg = g * 8 + 2 * nc4 + mat
                        if jg >= NWR:
                            ju = jg - NWR
                            g2, nc4p = ju // 8, (ju % 8) // 2
                            sync.wait("mm", f"guU{g2*16 + 4*nc4p + 3}")
                        slot = jg % NWR
                        dst3 = wrng[slot].ap().rearrange("p (c n) -> p c n", c=8)
                        for e in range(2):
                            sync.dma_start(
                                out=dst3[:, :, e * 256:(e + 1) * 256],
                                in_=pan[:, (pb + e) * P:(pb + e + 1) * P,
                                        nc4 * 256:(nc4 + 1) * 256]
                                    .transpose([1, 0, 2])).dma("sbd")
                        B.note(f"fgl{jg}", "sbd")
                sync.dma_start(out=fence_sb[:, :], in_=bin_d[:, 0:4]).dma("sbd")
                B.note(f"fguf{g}", "sbd")

            def emit_group_cc(g):
                gpsimd.wait("sdq", f"dgrp{g}")
                gpsimd.collective_compute(
                    "AllGather", Alu.bypass, replica_groups=[list(range(NCORES))],
                    ins=[deq_in[g].ap().opt()], outs=[ag_out[g].ap().opt()]).inc("scc")
                B.note(f"cc{g}", "scc")

            def emit_exchange():
                gpsimd.wait("v", "scans_done")
                gpsimd.dma_start(out=ex_in[:, :], in_=ex_sb[:, :]).dma("sexg")
                B.note("exin", "sexg")
                gpsimd.wait("sexg", "exin")
                gpsimd.collective_compute(
                    "AllGather", Alu.bypass, replica_groups=[list(range(NCORES))],
                    ins=[ex_in.ap().opt()], outs=[ex_out.ap().opt()]).inc("scc")
                B.note("ccex", "scc")
                gpsimd.wait("scc", "ccex")
                gpsimd.dma_start(
                    out=exa_sb.ap().rearrange("p (r k) -> p r k", r=NCORES),
                    in_=ex_out.ap().transpose([1, 0, 2])).dma("sexg")
                gpsimd.dma_start(out=fence_sb[:, :], in_=gs_d[0:8, 0:4]
                                 [:, None, :].broadcast_to([8, 16, 4])).dma("sexg")
                B.note("exall", "sexg")

            for i, sl in enumerate(deq_slices):
                if sl["fresh"]:
                    slot = sl["stage"] % 2
                    prev = sl["stage"] - 2
                    if prev >= 0:
                        gpsimd.wait("v", f"dv_stage{prev}")
                    if sl["kind"] == "gu":
                        g, s = sl["g"], sl["s"]
                        o = g * 1024 + s * TH
                        wsrc = wgs if sl["w"] == "wg" else wus
                        gpsimd.dma_start(out=dstage[slot][:, 0:TH],
                                         in_=wsrc[:, o:o + TH]).dma("spk")
                    else:
                        g = sl["g"]
                        gpsimd.dma_start(
                            out=dstage[slot][:, :],
                            in_=wds[g].rearrange("r (h dl) -> r h dl", h=2)
                                .transpose([1, 0, 2])).dma("spk")
                if i >= 2:
                    gpsimd.wait("v", f"dvex{i-2}")
                if sl["kind"] == "gu":
                    g, s = sl["g"], sl["s"]
                    o = g * 1024 + s * TH
                    ssrc = gs_d if sl["w"] == "wg" else us_d
                    gpsimd.dma_start(
                        out=dbc[i % 2][:, :],
                        in_=ssrc[:, o:o + TH][:, None, :].broadcast_to([8, 16, TH])
                    ).dma("spk")
                else:
                    g, s = sl["g"], sl["s"]
                    for h in range(2):
                        gpsimd.dma_start(
                            out=dbc[i % 2][h * 64:(h + 1) * 64, :],
                            in_=ds_d[g][:, h * 1024 + s * TH: h * 1024 + (s + 1) * TH]
                                [:, None, :].broadcast_to([4, 16, TH])
                        ).dma("spk")
                B.note(f"dload{i}", "spk")
                if i == NSL - 1:
                    gpsimd.dma_start(out=fence_sb[:, :], in_=gs_d[0:8, 0:4]
                                     [:, None, :].broadcast_to([8, 16, 4])).dma("spk")
                    B.note("dload_tail", "spk")
                if i >= 1:
                    gpsimd.wait("v", f"dvex{i-1}")
                    slp_ = deq_slices[i - 1]
                    for half in range(2):
                        gpsimd.dma_start(
                            out=dpan(slp_["g"], slp_["pbase"] + half)
                                [:, slp_["s"] * TH:(slp_["s"] + 1) * TH],
                            in_=dwout[half][:, :]).dma("sdq")
                    B.note(f"dwr{i-1}", "sdq")
                    if (i - 1) % 6 == 5:
                        gdone = (i - 1) // 6
                        B.note(f"dgrp{gdone}", "sdq")
                        emit_group_cc(gdone)
                    if i - 1 == 9:
                        emit_exchange()

            gpsimd.wait("v", f"dvex{NSL-1}")
            slp_ = deq_slices[NSL - 1]
            for half in range(2):
                gpsimd.dma_start(
                    out=dpan(slp_["g"], slp_["pbase"] + half)
                        [:, slp_["s"] * TH:(slp_["s"] + 1) * TH],
                    in_=dwout[half][:, :]).dma("sdq")
            B.note(f"dwr{NSL-1}", "sdq")
            B.note(f"dgrp{(NSL-1) // 6}", "sdq")
            emit_group_cc((NSL - 1) // 6)
            gpsimd.wait("v", "ffn_done")
            gpsimd.dma_start(
                out=out_d.ap().rearrange("(k p) t -> k p t", p=P).transpose([1, 0, 2]),
                in_=x_sb.ap().rearrange("p (k t) -> p k t", k=KC)).dma("sout")
            gpsimd.wait("sout", 16)
            gpsimd.wait("slp", "lspill_all")

            dq_i = [0]
            dq_seg = [0]

            def deq_seg(nseg):
                for _ in range(nseg):
                    i = dq_i[0]
                    if i >= NSL:
                        return
                    sl = deq_slices[i]
                    slot = sl["stage"] % 2
                    base = 0 if sl["kind"] == "gu" else sl["s"] * TH
                    s4 = dq_seg[0]
                    h, q = s4 // 2, s4 % 2
                    qs = slice(q * 256, (q + 1) * 256)
                    dsrc = dstage[slot][:, base + q * 256: base + (q + 1) * 256]
                    if s4 == 0:
                        vector.wait("spk", f"dload{i+1}" if i + 1 < NSL else "dload_tail")
                        if i >= 1:
                            vector.wait("sdq", f"dwr{i-1}")
                    kk = dklo[:, 0:256]
                    ss2 = dslo[:, 0:256]
                    if h == 0:
                        vector.tensor_scalar(kk, dsrc, 7, None, Alu.bitwise_and)
                        vector.tensor_scalar(ss2, dsrc, 8, None, Alu.bitwise_and)
                    else:
                        vector.tensor_scalar(kk, dsrc, 4, 7,
                                             Alu.logical_shift_right, Alu.bitwise_and)
                        vector.tensor_scalar(ss2, dsrc, 4, 8,
                                             Alu.logical_shift_right, Alu.bitwise_and)
                    t1 = dt1[:, 0:256]
                    u_ = du[:, 0:256]
                    w_ = dw_[:, 0:256]
                    qv = dq[:, 0:256]
                    vector.tensor_scalar(t1, kk, 0.5, None, Alu.mult)
                    vector.tensor_scalar(u_, t1, 2.0, -2.0, Alu.mult, Alu.add)
                    vector.tensor_scalar(w_, t1, 4.0, -8.0, Alu.mult, Alu.add)
                    vector.tensor_tensor(u_, t1, u_, Alu.max)
                    vector.tensor_tensor(u_, u_, w_, Alu.max)
                    vector.tensor_scalar(w_, ss2, -0.25, 1.0, Alu.mult, Alu.add)
                    vector.tensor_tensor(qv, u_, w_, Alu.mult)
                    vector.tensor_tensor(dwout[h][:, qs], qv,
                                         dbc[i % 2][:, qs], Alu.mult).inc("v")
                    if s4 == 3:
                        B.note(f"dvex{i}", "v")
                        if sl["kind"] == "gu" or sl["s"] == 1:
                            B.note(f"dv_stage{sl['stage']}", "v")
                        dq_seg[0] = 0
                        dq_i[0] += 1
                    else:
                        dq_seg[0] = s4 + 1

            vector.memset(ones_k[:, :], 1.0)
            vector.memset(ones_1[:, :], 1.0)
            vector.memset(zeros_t[:, :], 0.0)
            vector.memset(S_sb[:, :], 0.0).inc("v")
            B.note("v_init", "v")
            for kc in range(KC):
                vector.wait("sx", f"sx{kc+1}" if kc + 1 < KC else "sx_all")
                if kc >= 1:
                    vector.wait("mm", f"lnmm{kc-1}")
                vector.tensor_copy(xbf[0][:, :], xch(kc)).inc("v")
                B.note(f"lncp{kc}", "v")
            vector.wait("sx", "sx_all")
            vector.tensor_tensor(bds_sb[:, :], bd_sb[:, :], db_sb[:, :], Alu.add).inc("v")
            B.note("v_bds", "v")
            deq_seg(4)
            vector.wait("mm", "mm_stats")
            for th in range(2):
                slc = slice(th * TH, (th + 1) * TH)
                vector.tensor_scalar(ta_s[:, slc], psS4[th][0:1, :], 1.0 / D, None, Alu.mult)
                vector.tensor_scalar(vs_row[:, slc], psS4[2 + th][0:1, :], 1.0 / D, EPS,
                                     Alu.mult, Alu.add)
            for th in range(2):
                slc = slice(th * TH, (th + 1) * TH)
                vector.tensor_tensor(sq_r[th], ta_s[:, slc], ta_s[:, slc], Alu.mult)
                vector.tensor_tensor(var_r[th], vs_row[:, slc], sq_r[th], Alu.subtract)
            ch = None
            for th in range(2):
                ch = vector.reciprocal(sq_r[th], var_r[th])
            ch.inc("v")
            B.note("v_lnstat", "v")
            vector.wait("se", "se_rstd")
            vector.tensor_tensor(mr_bf[:, :], ta_s[:, :], rstd_bf[:, :], Alu.mult).inc("v")
            B.note("v_mr", "v")
            vector.wait("se", "se_bc")
            for kc in range(KC):
                vector.tensor_tensor(em_t[:, :], xch(kc), rstd_b[:, :], Alu.mult)
                vector.tensor_tensor(xnch(kc), em_t[:, :], mrr_b[:, :], Alu.subtract).inc("v")
                B.note(f"xn{kc}", "v")
            B.note("v_xn", "v")
            for ec in range(KC):
                par = ec % 2
                vector.wait("se", f"ecse{ec}")
                if ec >= 2:
                    vector.wait("slp", f"lspill{ec-2}")
                vector.tensor_tensor(em_t[:, :], dec_t[par][:, :], inp_t[par][:, :], Alu.mult)
                vector.tensor_tensor(em_t[:, :], inp_t[par][:, :], em_t[:, :], Alu.subtract)
                vector.tensor_tensor_scan(Lt[par][:, :], dec_t[par][:, :], em_t[:, :], 0.0,
                                          Alu.mult, Alu.add)
                vector.tensor_tensor_scan(Pt[par][:, :], dec_t[par][:, :], zeros_t[:, :], 1.0,
                                          Alu.mult, Alu.add).inc("v")
                B.note(f"scandone{ec}", "v")
                vector.wait("v", f"scandone{ec}")
                vector.tensor_copy(ex_sb[:, ec:ec + 1], Pt[par][:, T - 1:T])
                vector.tensor_copy(ex_sb[:, KC + ec:KC + ec + 1],
                                   Lt[par][:, T - 1:T]).inc("v")
                B.note(f"ecv{ec}", "v")
                deq_seg(3 if ec < 4 else 2)
            B.note("scans_done", "v")
            vector.wait("sexg", "exall")
            for j in range(NCORES - 1):
                A_j = exa_sb[:, 32 * j:32 * j + 16]
                B_j = exa_sb[:, 32 * j + 16:32 * j + 32]
                vector.tensor_tensor(Tt_sb[:, :], A_j, S_sb[:, :], Alu.mult)
                vector.tensor_tensor(Tt_sb[:, :], Tt_sb[:, :], B_j, Alu.add)
                vector.tensor_tensor(Dt_sb[:, :], Tt_sb[:, :], S_sb[:, :], Alu.subtract)
                vector.tensor_scalar(Dt_sb[:, :], Dt_sb[:, :], msk_sb[:, j:j + 1],
                                     None, Alu.mult)
                ch = vector.tensor_tensor(S_sb[:, :], S_sb[:, :], Dt_sb[:, :], Alu.add)
            ch.inc("v")
            B.note("v_prefix", "v")
            for ec in range(KC):
                par = ec % 2
                vector.wait("se", f"ecgse{ec}")
                vector.wait("slpr", f"lpre{min(ec + 1, KC - 1)}")
                if ec == KC - 1:
                    vector.wait("swa", "wload49")
                vector.scalar_tensor_tensor(em_t[:, :], Pt[par][:, :], S_sb[:, ec:ec + 1],
                                            Lt[par][:, :], Alu.mult, Alu.add)
                vector.tensor_tensor(ych(ec), gtmp[par][:, :], em_t[:, :], Alu.mult).inc("v")
                B.note(f"ecy{ec}", "v")
                deq_seg(1)
            B.note("v_y", "v")
            for dc in range(KC):
                vector.wait("mm", f"dcmm{dc}")
                for th in range(2):
                    xsl = x_sb[:, dc * T + th * TH: dc * T + th * TH + TH]
                    vector.scalar_tensor_tensor(xsl, psO4[dc % 2][th][:, :],
                                                bo_sb[:, dc:dc + 1],
                                                xsl, Alu.add, Alu.add).inc("v")
                B.note(f"dcv{dc}", "v")
                deq_seg(1)
            B.note("v_attn_done", "v")

            for g in range(NGRP):
                for u in range(16):
                    idx = g * 16 + u
                    deq_seg(1)
                    vector.wait("se", f"gel{idx}")
                    vector.wait("mm", f"guU{idx}")
                    nc8, th = u // 2, u % 2
                    ioff = nc8 * T + th * TH
                    vector.tensor_tensor(inter[:, ioff:ioff + TH], gel_t[idx % 2][:, :],
                                         psU[idx % 2][:, :], Alu.mult).inc("v")
                    B.note(f"intr{idx}", "v")
                for dc in range(KC):
                    vector.wait("mm", f"dn{g}_{dc}")
                    ch = None
                    for th in range(2):
                        xsl = x_sb[:, dc * T + th * TH: dc * T + th * TH + TH]
                        ch = vector.tensor_tensor(xsl, xsl, psDn4[dc % 2][th][:, :],
                                                  Alu.add)
                    ch.inc("v")
                    B.note(f"xadd{g}_{dc}", "v")
                    if dc % 2 == 1:
                        deq_seg(1)
                B.note(f"gxadd{g}", "v")
            deq_seg(4 * NSL)
            B.note("ffn_done", "v")

            for kc in range(KC):
                scalar.wait("sx", f"sx{kc+1}" if kc + 1 < KC else "sx_all")
                if kc >= 1:
                    scalar.wait("mm", f"lnmm{kc-1}")
                scalar.activation(xsq[0][:, :], xch(kc), Act.Square).inc("se")
                B.note(f"lnsq{kc}", "se")
            scalar.wait("v", "v_lnstat")
            ch = None
            for th in range(2):
                ch = scalar.activation(rstd_bf[:, th * TH:(th + 1) * TH],
                                       sq_r[th], Act.Sqrt)
            ch.inc("se")
            B.note("se_rstd", "se")
            for th in range(2):
                slc = slice(th * TH, (th + 1) * TH)
                scalar.wait("mm", f"mmbc{th}")
                scalar.copy(rstd_b[:, slc], psBC[0][:, :])
                scalar.copy(mrr_b[:, slc], psBC[1][:, :]).inc("se", 2)
                B.note(f"se_bct{th}", "se")
            B.note("se_bc", "se")
            wconv = [0]

            def conv_steps(upto):
                while wconv[0] < min(upto, NWS):
                    i = wconv[0]
                    scalar.wait("swa", f"wload{i+1}" if i + 1 < NWS else "wl_tail")
                    if i >= 3:
                        scalar.wait("mm", f"wuse{i-3}")
                    scalar.copy(wbf[i % 3][:, :], wst[i % 2][:, :]).inc("se")
                    B.note(f"wconv{i}", "se")
                    wconv[0] += 1

            conv_steps(2)
            for ec in range(KC):
                par = ec % 2
                scalar.wait("mm", f"ecmmi{ec}")
                if ec >= 2:
                    scalar.wait("v", f"ecv{ec-2}")
                for th in range(2):
                    scalar.activation(inp_t[par][:, th * TH:(th + 1) * TH],
                                      psI4[par][th][:, :],
                                      Act.Silu, bias=bin_sb[:, ec:ec + 1], scale=1.0)
                scalar.wait("mm", f"ecmmd{ec}")
                scalar.wait("v", "v_bds")
                ch = None
                for th in range(2):
                    ch = scalar.activation(dec_t[par][:, th * TH:(th + 1) * TH],
                                           psDc4[par][th][:, :],
                                           Act.Sigmoid, bias=bds_sb[:, ec:ec + 1], scale=1.0)
                ch.inc("se")
                B.note(f"ecse{ec}", "se")
                if ec >= 1:
                    scalar.wait("v", f"ecv{ec-1}")
                    scalar.dma_start(out=lp_dram[1, ec - 1], in_=Lt[(ec - 1) % 2][:, :]).dma("slp")
                    scalar.dma_start(out=lp_dram[0, ec - 1], in_=Pt[(ec - 1) % 2][:, :]).dma("slp")
                    B.note(f"lspill{ec-1}", "slp")
                conv_steps(min(2 * ec + 4, 34))
            scalar.wait("v", f"ecv{KC-1}")
            scalar.dma_start(out=lp_dram[1, KC - 1], in_=Lt[(KC - 1) % 2][:, :]).dma("slp")
            scalar.dma_start(out=lp_dram[0, KC - 1], in_=Pt[(KC - 1) % 2][:, :]).dma("slp")
            B.note(f"lspill{KC-1}", "slp")
            B.note("lspill_all", "slp")
            for ec in range(KC):
                conv_steps(min(2 * KC + ec + 3, 47))
                scalar.wait("mm", f"ecgmm{ec}")
                if ec >= 2:
                    scalar.wait("v", f"ecy{ec-2}")
                ch = None
                for th in range(2):
                    ch = scalar.activation(gtmp[ec % 2][:, th * TH:(th + 1) * TH],
                                           psI4[ec % 2][th][:, :],
                                           Act.Sigmoid, bias=bg_sb[:, ec:ec + 1], scale=1.0)
                ch.inc("se")
                B.note(f"ecgse{ec}", "se")
                conv_steps(min(2 * KC + ec + 4, 48 if ec >= 13 else 47))
            conv_steps(NWS)
            for g in range(NGRP):
                if g == 0:
                    scalar.wait("v", "v_attn_done")
                for idx in range(g * 16, g * 16 + 16):
                    scalar.wait("mm", f"gu{idx}")
                    if idx >= 2:
                        scalar.wait("v", f"intr{idx-2}")
                    scalar.activation(gel_t[idx % 2][:, :], psG[idx % 2][:, :],
                                      Act.Gelu_apprx_tanh).inc("se")
                    B.note(f"gel{idx}", "se")
                    if idx == g * 16 + 1:
                        scalar.wait("scc", f"cc{g}")
                        if g >= 1:
                            scalar.wait("mm", f"duse{g-1}")
                        for j8 in range(8):
                            qq, e = j8 // 2, j8 % 2
                            for half in range(2):
                                rank = 2 * qq + half
                                scalar.dma_start(
                                    out=Wdb[half * 64:(half + 1) * 64,
                                            j8 * D:(j8 + 1) * D],
                                    in_=agpan(g, rank, 4 + e)
                                        .rearrange("(h r) dl -> h r dl", h=2)
                                        .transpose([1, 0, 2])).dma("slp")
                        scalar.dma_start(out=fence_sb[:, :], in_=bin_d[:, 0:4]).dma("slp")
                        B.note(f"fwdf{g}", "slp")

            for kc in range(KC):
                tensor.wait("v", f"lncp{kc}")
                tensor.wait("se", f"lnsq{kc}")
                ch = None
                for th in range(2):
                    tensor.matmul(psS4[th][0:1, :], ones_k[:, :],
                                  xbf[0][:, th * TH:(th + 1) * TH],
                                  start=(kc == 0), stop=(kc == KC - 1))
                    ch = tensor.matmul(psS4[2 + th][0:1, :], ones_k[:, :],
                                       xsq[0][:, th * TH:(th + 1) * TH],
                                       start=(kc == 0), stop=(kc == KC - 1))
                ch.inc("mm")
                B.note(f"lnmm{kc}", "mm")
            B.note("mm_stats", "mm")
            tensor.wait("v", "v_mr")
            for th in range(2):
                slc = slice(th * TH, (th + 1) * TH)
                if th == 1:
                    tensor.wait("se", "se_bct0")
                tensor.matmul(psBC[0][:, :], ones_1[:, :], rstd_bf[:, slc],
                              start=True, stop=True)
                tensor.matmul(psBC[1][:, :], ones_1[:, :], mr_bf[:, slc],
                              start=True, stop=True).inc("mm")
                B.note(f"mmbc{th}", "mm")
            for ec in range(KC):
                ci, cd = 2 * ec, 2 * ec + 1
                par = ec % 2
                if ec == 0:
                    tensor.wait("se", "se_bc")
                else:
                    tensor.wait("v", "v_xn")
                tensor.wait("se", f"wconv{ci}")
                if ec >= 2:
                    tensor.wait("se", f"ecse{ec-2}")
                ch = None
                for kc in range(KC):
                    if ec == 0:
                        tensor.wait("v", f"xn{kc}")
                    for th in range(2):
                        ch = tensor.matmul(psI4[par][th][:, :],
                                           wbf[ci % 3][:, kc * P:(kc + 1) * P],
                                           xnch(kc)[:, th * TH:(th + 1) * TH],
                                           start=(kc == 0), stop=(kc == KC - 1))
                ch.inc("mm")
                B.note(f"ecmmi{ec}", "mm")
                B.note(f"wuse{ci}", "mm")
                tensor.wait("se", f"wconv{cd}")
                if ec == 0:
                    tensor.wait("v", "v_xn")
                ch = None
                for kc in range(KC):
                    for th in range(2):
                        ch = tensor.matmul(psDc4[par][th][:, :],
                                           wbf[cd % 3][:, kc * P:(kc + 1) * P],
                                           xnch(kc)[:, th * TH:(th + 1) * TH],
                                           start=(kc == 0), stop=(kc == KC - 1))
                ch.inc("mm")
                B.note(f"ecmmd{ec}", "mm")
                B.note(f"wuse{cd}", "mm")
            for ec in range(KC):
                cg = 2 * KC + ec
                tensor.wait("se", f"wconv{cg}")
                if ec >= 2:
                    tensor.wait("se", f"ecgse{ec-2}")
                else:
                    tensor.wait("se", f"ecse{KC-1}")
                ch = None
                for kc in range(KC):
                    for th in range(2):
                        ch = tensor.matmul(psI4[ec % 2][th][:, :],
                                           wbf[cg % 3][:, kc * P:(kc + 1) * P],
                                           xnch(kc)[:, th * TH:(th + 1) * TH],
                                           start=(kc == 0), stop=(kc == KC - 1))
                ch.inc("mm")
                B.note(f"ecgmm{ec}", "mm")
                B.note(f"wuse{cg}", "mm")
            for dc in range(KC):
                co = 3 * KC + dc
                tensor.wait("se", f"wconv{co}")
                tensor.wait("v", "v_y")
                if dc >= 2:
                    tensor.wait("v", f"dcv{dc-2}")
                ch = None
                for ec in range(KC):
                    for th in range(2):
                        ch = tensor.matmul(psO4[dc % 2][th][:, :],
                                           wbf[co % 3][:, ec * P:(ec + 1) * P],
                                           ych(ec)[:, th * TH:(th + 1) * TH],
                                           start=(ec == 0), stop=(ec == KC - 1))
                ch.inc("mm")
                B.note(f"dcmm{dc}", "mm")
                B.note(f"wuse{co}", "mm")
            for g in range(NGRP):
                for nc8 in range(8):
                    nc4, h128 = nc8 // 2, nc8 % 2
                    slot_g = (g * 8 + 2 * nc4) % NWR
                    slot_u = (g * 8 + 2 * nc4 + 1) % NWR
                    idx0 = g * 16 + nc8 * 2
                    idx1 = idx0 + 1
                    p0, p1 = idx0 % 2, idx1 % 2
                    if h128 == 0:
                        key = (f"fgl{g*8 + 2*nc4 + 2}" if nc4 < 3
                               else f"fguf{g}")
                        tensor.wait("sbd", key)
                    if idx0 >= 2:
                        tensor.wait("se", f"gel{idx0-2}")
                        tensor.wait("se", f"gel{idx1-2}")
                    else:
                        tensor.wait("v", "v_attn_done")
                    coff = h128 * P
                    ch = None
                    for kc in range(KC):
                        for th, pp in ((0, p0), (1, p1)):
                            ch = tensor.matmul(psG[pp][:, :],
                                               wrng[slot_g][:, kc * 256 + coff:
                                                            kc * 256 + coff + P],
                                               xnch(kc)[:, th * TH:(th + 1) * TH],
                                               start=(kc == 0), stop=(kc == KC - 1))
                    ch.inc("mm")
                    B.note(f"gu{idx0}", "mm")
                    B.note(f"gu{idx1}", "mm")
                    if idx0 >= 2:
                        tensor.wait("v", f"intr{idx0-2}")
                        tensor.wait("v", f"intr{idx1-2}")
                    ch = None
                    for kc in range(KC):
                        for th, pp in ((0, p0), (1, p1)):
                            ch = tensor.matmul(psU[pp][:, :],
                                               wrng[slot_u][:, kc * 256 + coff:
                                                            kc * 256 + coff + P],
                                               xnch(kc)[:, th * TH:(th + 1) * TH],
                                               start=(kc == 0), stop=(kc == KC - 1))
                    ch.inc("mm")
                    B.note(f"guU{idx0}", "mm")
                    B.note(f"guU{idx1}", "mm")
                tensor.wait("v", f"intr{g*16 + 15}")
                tensor.wait("slp", f"fwdf{g}")
                for dc in range(KC):
                    if dc >= 2:
                        tensor.wait("v", f"xadd{g}_{dc-2}")
                    elif g >= 1:
                        tensor.wait("v", f"xadd{g-1}_{14+dc}")
                    ch = None
                    for j8 in range(8):
                        for th in range(2):
                            ch = tensor.matmul(psDn4[dc % 2][th][:, :],
                                               Wdb[:, j8 * D + dc * P: j8 * D + dc * P + P],
                                               inter[:, j8 * T + th * TH: j8 * T + th * TH + TH],
                                               start=(j8 == 0), stop=(j8 == 7))
                    ch.inc("mm")
                    B.note(f"dn{g}_{dc}", "mm")
                B.note(f"duse{g}", "mm")

        rec = {}
        B1 = _Builder(False, rec)
        emit(B1)
        B2 = _Builder(True, rec)
        emit(B2)

        def flush(engname):
            stream = B2.stream[engname]

            def body(eng):
                for it in stream:
                    if it[0] == "w":
                        eng.wait_ge(sems[it[1]], it[2])
                    else:
                        _, name, args, kwargs, ri = it
                        instr = getattr(eng, name)(*args, **kwargs)
                        for s, k in ri.incs:
                            instr.then_inc(sems[s], k)
            return body

        block.sync(flush("sync"))
        block.vector(flush("vector"))
        block.scalar(flush("scalar"))
        block.tensor(flush("tensor"))
        block.gpsimd(flush("gpsimd"))

    return nc


_PROGRAM_CACHE = {}


def _get_program():
    if "nc" not in _PROGRAM_CACHE:
        _PROGRAM_CACHE["nc"] = build_program()
    return _PROGRAM_CACHE["nc"]


def _perm_interleave(n):
    out = np.empty(n, np.int64)
    c = np.arange(n // 256)[:, None, None]
    e = np.arange(2)[None, :, None]
    p = np.arange(P)[None, None, :]
    idx = 128 * (2 * c + e) + p
    val = 256 * c + 2 * p + e
    out[idx.ravel()] = val.ravel()
    return out


def kernel(**inputs):
    SIG = _perm_interleave(D)
    TAU = _perm_interleave(NMLP)

    x = np.asarray(inputs["L0_attn_x"])[0]
    W_in = np.asarray(inputs["L0_attn_W_in"])
    W_dec = np.asarray(inputs["L0_attn_W_decay"])
    W_gate = np.asarray(inputs["L0_attn_W_gate"])
    W_out = np.asarray(inputs["L0_attn_W_out"])

    def tile_kx(W):
        return np.ascontiguousarray(
            W.reshape(KC, P, KC, P).transpose(2, 1, 0, 3)).astype(np.float32)

    wti_h = tile_kx(W_in[SIG])
    wtd_h = tile_kx(W_dec[SIG])
    wtg_h = tile_kx(W_gate[SIG])
    wto_h = tile_kx(W_out[:, SIG])

    tb = lambda b: np.ascontiguousarray(np.asarray(b).reshape(KC, P).T).astype(np.float32)
    bin_h = tb(inputs["L0_attn_b_in"])
    bg_h = tb(inputs["L0_attn_b_gate"])
    bd_h = tb(inputs["L0_attn_b_decay"])
    db_h = tb(inputs["L0_attn_decay_bias"])
    bo_h = tb(np.asarray(inputs["L0_attn_b_out"])[SIG])

    u8v = lambda a, n, kh: np.asarray(a).view(np.uint8).reshape(n, kh, 4)[:, :, 0]
    wg_u8 = u8v(inputs["L0_ffn_gate_W_packed"], NMLP, 1024)
    wu_u8 = u8v(inputs["L0_ffn_up_W_packed"], NMLP, 1024)
    wd_u8 = u8v(inputs["L0_ffn_down_W_packed"], D, 8192)
    WgT = np.ascontiguousarray(wg_u8[TAU].T)
    WuT = np.ascontiguousarray(wu_u8[TAU].T)
    WdT = np.ascontiguousarray(wd_u8[SIG].T)
    gsT = np.ascontiguousarray(np.asarray(inputs["L0_ffn_gate_scales"])[TAU].T).astype(np.float32)
    usT = np.ascontiguousarray(np.asarray(inputs["L0_ffn_up_scales"])[TAU].T).astype(np.float32)
    dsT = np.ascontiguousarray(np.asarray(inputs["L0_ffn_down_scales"])[SIG].T).astype(np.float32)

    xT = np.ascontiguousarray(x.T[SIG])

    in_maps = []
    for c in range(NCORES):
        rows = np.concatenate([np.arange(512 * g + 64 * c, 512 * g + 64 * c + 64)
                               for g in range(NGRP)])
        wds_h = np.ascontiguousarray(WdT[rows].reshape(NGRP, 64, D))
        dsrows = np.concatenate([np.arange(32 * g + 4 * c, 32 * g + 4 * c + 4)
                                 for g in range(NGRP)])
        ds_h = np.ascontiguousarray(dsT[dsrows].reshape(NGRP, 4, D))
        msk_h = np.tile((np.arange(8) < c).astype(np.float32), (P, 1))
        in_maps.append(dict(
            xin=np.ascontiguousarray(xT[:, c * T:(c + 1) * T]),
            wti=wti_h, wtd=wtd_h, wtg=wtg_h, wto=wto_h,
            bin_d=bin_h, bg_d=bg_h, bd_d=bd_h, db_d=db_h, bo_d=bo_h,
            wgs=np.ascontiguousarray(WgT[128 * c:128 * (c + 1)]),
            wus=np.ascontiguousarray(WuT[128 * c:128 * (c + 1)]),
            wds=wds_h,
            gs_d=np.ascontiguousarray(gsT[8 * c:8 * c + 8]),
            us_d=np.ascontiguousarray(usT[8 * c:8 * c + 8]),
            ds_d=ds_h,
            msk_d=np.ascontiguousarray(msk_h),
        ))

    nc = _get_program()
    res = run_bass_kernel_spmd(nc, in_maps, core_ids=list(range(NCORES)))

    out_dev = np.concatenate([np.asarray(res.results[c]["out_d"]) for c in range(NCORES)],
                             axis=1)
    out = np.empty_like(out_dev)
    out[SIG] = out_dev
    return np.ascontiguousarray(out.T)[None].astype(np.float32)
